# version 16
# baseline (speedup 1.0000x reference)
import numpy as np
from contextlib import ExitStack

import concourse.bass as bass
import concourse.mybir as mybir
from concourse.bass_utils import run_bass_kernel_spmd

F32 = mybir.dt.float32
BF16 = mybir.dt.bfloat16
I32 = mybir.dt.int32
AF = mybir.ActivationFunctionType
ALU = mybir.AluOpType
AX = mybir.AxisListType

NSEQ = 4
T = 2048
D = 1024
E = 512
H = 3072
NT = 16
DC = 8
HB = 24
SCALE = float(E) ** -0.5
EPS = 1e-5
MASKVAL = -1.0e30
DEQ = 1.0 / 127.0

ENGINES = ("sync", "scalar", "vector", "tensor", "gpsimd")
DMA_SEMS = ("LD", "GL", "ST")


class Em:

    def __init__(self):
        self.ops = {e: [] for e in ENGINES}
        self.cnt = {}

    def n(self, sem):
        return self.cnt.get(sem, 0)

    def op(self, eng, fn, waits=(), inc=None, amt=1, kind="op"):
        w = {}
        for t in waits:
            if t is None:
                continue
            s, v = t
            assert v <= self.cnt.get(s, 0), f"wait on future token {s}={v}"
            w[s] = max(w.get(s, 0), v)
        tok = None
        if inc is not None:
            self.cnt[inc] = self.cnt.get(inc, 0) + amt
            tok = (inc, amt)
            ret = (inc, self.cnt[inc])
        else:
            ret = None
        self.ops[eng].append((sorted(w.items()), fn, tok, kind))
        return ret


def build_nc():
    nc = bass.Bass("TRN2", target_bir_lowering=False)

    x_in = nc.dram_tensor("x_in", [NSEQ, T, D], F32, kind="ExternalInput").ap()
    w_qkv = nc.dram_tensor("w_qkv", [3 * E, D], F32, kind="ExternalInput").ap()
    w_out = nc.dram_tensor("w_out", [D, E], F32, kind="ExternalInput").ap()
    b_out = nc.dram_tensor("b_out", [1, D], F32, kind="ExternalInput").ap()
    w_gate = nc.dram_tensor("w_gate", [H, D], I32, kind="ExternalInput").ap()
    w_up = nc.dram_tensor("w_up", [H, D], I32, kind="ExternalInput").ap()
    w_down = nc.dram_tensor("w_down", [D, H], I32, kind="ExternalInput").ap()
    out = nc.dram_tensor("out", [NSEQ, T, D], F32, kind="ExternalOutput").ap()

    scr_g = nc.dram_tensor("scr_g", [HB, 128, DC * 128], BF16, kind="Internal").ap()
    scr_u = nc.dram_tensor("scr_u", [HB, 128, DC * 128], BF16, kind="Internal").ap()
    scr_d = nc.dram_tensor("scr_d", [HB, 128, D], BF16, kind="Internal").ap()
    scr_xa = nc.dram_tensor("scr_xa", [NT, 128, D], F32, kind="Internal").ap()

    es = ExitStack()
    with es:
        sb = lambda *a, **k: es.enter_context(nc.sbuf_tensor(*a, **k))
        A = sb("arena", [128, HB * 1024], BF16)
        xn_fm = sb("xn_fm", [128, DC * T], BF16)
        k_fm = sb("k_fm", [128, 4 * T], BF16)
        v_tm = sb("v_tm", [128, NT * E], BF16)
        q_grp = sb("q_grp", [128, 4 * 256], BF16)
        attn_q = sb("attn_q", [128, 2 * T], BF16)
        attnT_g = sb("attnT_g", [128, NT * 256], BF16)
        av_fm = sb("av_fm", [128, 4 * 256], BF16)
        xn_stage = sb("xn_stage", [128, 4 * D], BF16)
        wg_buf = sb("wg_buf", [128, 4 * 1024], BF16)
        wu_buf = sb("wu_buf", [128, 4 * 1024], BF16)
        wd_buf = sb("wd_buf", [128, 4 * 1024], BF16)
        sil_buf = sb("sil_buf", [128, 2 * 512], BF16)
        ident = sb("ident", [128, 128], BF16)
        zeros_bf = sb("zeros_bf", [128, 128], BF16)
        ones_bf = sb("ones_bf", [1, 128], BF16)
        b_row_bf = sb("b_row_bf", [1, D], BF16)

        x_stage = sb("x_stage", [128, 3 * D], F32)
        xa_stage = sb("xa_stage", [128, 2 * D], F32)
        out_stage = sb("out_stage", [128, 2 * D], F32)
        junk = sb("junk", [128, D], F32)
        diag = sb("diag", [128, 128], F32)
        b_row_f = sb("b_row_f", [1, D], F32)
        stats = sb("stats", [128, 16], F32)
        mxp = sb("mxp", [128, 8], F32)
        rsp = sb("rsp", [128, 8], F32)
        smax = sb("smax", [128, 4], F32)
        srec = sb("srec", [128, 4], F32)
        epsv = sb("epsv", [128, 1], F32)

        psB3 = [es.enter_context(nc.psum_tensor(f"psB{i}", [128, 1024], F32)) for i in range(3)]
        psF = [psB3[i // 2][:, (i % 2) * 512:(i % 2) * 512 + 512] for i in range(6)]
        psT = [es.enter_context(nc.psum_tensor(f"psT{i}", [128, 1024], BF16)) for i in range(2)]

        sems = {}
        for name in ("PE", "VE", "SE", "GE"):
            sems[name] = es.enter_context(nc.semaphore(name))
        N_LANES = {"LD": 8, "ST": 4, "GL": 2}
        for base, n in N_LANES.items():
            for i in range(n):
                sems[f"{base}{i}"] = es.enter_context(nc.semaphore(f"{base}{i}"))
        lane_rr = {"LD": 0, "ST": 0, "GL": 0}

        em = Em()

        def dma(queue, dst, src, waits=(), sem=None):
            if sem is None:
                sem = "ST" if queue == "gpsimd" else "LD"
            lane = lane_rr[sem]
            lane_rr[sem] = (lane + 1) % N_LANES[sem]
            return em.op(queue, lambda e, d=dst, s=src: e.dma_start(out=d, in_=s),
                         waits, inc=f"{sem}{lane}", amt=16, kind="dma")

        evac_toggle = [0]

        def evac(dst, src, waits, prefer=None):
            which = prefer
            if which is None:
                which = "vector" if evac_toggle[0] == 0 else "scalar"
                evac_toggle[0] ^= 1
            if which == "vector":
                return em.op("vector", lambda e, d=dst, s=src: e.tensor_copy(d, s),
                             waits, inc="VE")
            return em.op("scalar", lambda e, d=dst, s=src: e.copy(d, s),
                         waits, inc="SE")

        fence_first = [True]

        def pe_fence(bank_ap):
            w = [tok_zeros, tok_masks] if fence_first[0] else []
            fence_first[0] = False
            return em.op("tensor", lambda e, b=bank_ap: e.matmul(
                b, zeros_bf[:], ident[:],
                start=False, stop=True, skip_group_check=True), w, inc="PE")

        psT_free = [None, None]
        psT_i = [0]

        def transpose_batch(jobs):
            toks = []
            for i0 in range(0, len(jobs), 2):
                pair = jobs[i0:i0 + 2]
                slots = []
                for dst, src, w in pair:
                    slot = psT_i[0] % 2
                    psT_i[0] += 1
                    ww = list(w) + [psT_free[slot]]
                    em.op("tensor",
                          lambda e, s=src, p=psT[slot]: e.transpose(p[:, 0:128], s, ident[:]),
                          ww, inc="PE")
                    slots.append(slot)
                t_dr = em.op("tensor", lambda e: e.drain(), [], inc="PE")
                for (dst, src, w), slot in zip(pair, slots):
                    et = evac(dst, psT[slot][:, 0:128], [t_dr])
                    psT_free[slot] = et
                    toks.append(et)
            return toks

        def transpose_to(dst, src_ap, waits):
            return transpose_batch([(dst, src_ap, waits)])[0]

        psF_free = [None] * 6
        psf_i = [0]

        def psf_slot():
            slot = psf_i[0] % 6
            psf_i[0] += 1
            return slot

        em.op("gpsimd", lambda e: e.memset(ident[:], 0.0), inc="GE")
        em.op("gpsimd", lambda e: e.affine_select(
            out=ident[:], in_=ident[:], compare_op=ALU.not_equal, fill=1.0,
            base=0, pattern=[[-1, 128]], channel_multiplier=1), inc="GE")
        em.op("gpsimd", lambda e: e.memset(diag[:], 0.0), inc="GE")
        tok_masks = em.op("gpsimd", lambda e: e.affine_select(
            out=diag[:], in_=diag[:], compare_op=ALU.is_ge, fill=MASKVAL,
            base=0, pattern=[[-1, 128]], channel_multiplier=1), inc="GE")
        tok_ones = em.op("vector", lambda e: e.memset(ones_bf[:], 1.0), inc="VE")
        tok_zeros = em.op("vector", lambda e: e.memset(zeros_bf[:], 0.0), inc="VE")
        tok_eps = em.op("vector", lambda e: e.memset(epsv[:], EPS), inc="VE")

        tok = dma("sync", b_row_f[:], b_out)
        tok_brow = em.op("vector", lambda e: e.tensor_copy(b_row_bf[:], b_row_f[:]),
                         [tok], inc="VE")

        ldd = []
        for ob in range(8):
            ldd.append(dma("gpsimd", A[:, ob * 3072:(ob + 1) * 3072],
                           w_down[ob * 128:(ob + 1) * 128, :], waits=[tok_masks] if ob == 0 else [], sem="GL"))
        xnfree = [[], [], [], []]
        for hc in range(HB):
            p = hc % 4
            jobs = []
            for ob in range(8):
                src = A[:, ob * 3072 + hc * 128: ob * 3072 + hc * 128 + 128]
                dst = k_fm[:, p * 1024 + ob * 128: p * 1024 + ob * 128 + 128]
                jobs.append((dst, src, [ldd[ob]] + xnfree[p]))
            evs = transpose_batch(jobs)
            st = dma("sync", scr_d[hc], k_fm[:, p * 1024:(p + 1) * 1024],
                     waits=evs, sem="ST")
            xnfree[p] = [st]
        wd_done = sum(xnfree, [])

        def nat12(blk):
            if blk < 8:
                return k_fm[:, blk * 1024:(blk + 1) * 1024]
            return v_tm[:, (blk - 8) * 1024:(blk - 7) * 1024]
        ldtoks = []
        for blk in range(12):
            ldtoks.append(dma("gpsimd", nat12(blk),
                              w_qkv[blk * 128:(blk + 1) * 128, :],
                              waits=wd_done if blk < 4 else [], sem="GL"))
        ev = []
        for eb in range(12):
            jobs = []
            for dc in range(DC):
                src = nat12(eb)[:, dc * 128: dc * 128 + 128]
                dst = A[:, dc * 1536 + eb * 128: dc * 1536 + eb * 128 + 128]
                jobs.append((dst, src, [ldtoks[eb]]))
            ev += transpose_batch(jobs)
        w1 = [ev[-1]]
        ldo = []
        for ob in range(8):
            ldo.append(dma("gpsimd", v_tm[:, 4096 + ob * 512: 4096 + (ob + 1) * 512],
                           w_out[ob * 128:(ob + 1) * 128, :], waits=w1 + ev, sem="GL"))
            w1 = []
            ev = []
        evo = []
        for eb in range(4):
            jobs = []
            for ob in range(8):
                src = v_tm[:, 4096 + ob * 512 + eb * 128: 4096 + ob * 512 + eb * 128 + 128]
                dst = A[:, 12288 + eb * 1024 + ob * 128: 12288 + eb * 1024 + ob * 128 + 128]
                jobs.append((dst, src, [ldo[ob]]))
            evo += transpose_batch(jobs)
        natfree = [wd_done for _ in range(4)]
        tfree = [[] for _ in range(4)]
        for wsrc, wscr in ((w_gate, scr_g), (w_up, scr_u)):
            for hb in range(HB):
                p = hb % 4
                lt = dma("gpsimd", A[:, 16384 + p * 1024: 16384 + (p + 1) * 1024],
                         wsrc[hb * 128:(hb + 1) * 128, :], waits=natfree[p], sem="GL")
                jobs = []
                for dc in range(DC):
                    src = A[:, 16384 + p * 1024 + dc * 128: 16384 + p * 1024 + dc * 128 + 128]
                    dst = A[:, 20480 + p * 1024 + dc * 128: 20480 + p * 1024 + dc * 128 + 128]
                    jobs.append((dst, src, [lt] + tfree[p]))
                evs = transpose_batch(jobs)
                st = dma("sync", wscr[hb], A[:, 20480 + p * 1024: 20480 + (p + 1) * 1024],
                         waits=evs, sem="ST")
                natfree[p] = evs
                tfree[p] = [st]

        arena_users = sum(natfree, []) + sum(tfree, [])
        x_stage_free = [[], [], []]
        xa_stage_free = [[], []]
        out_stage_free = [[], []]
        wg_free = [[], [], [], []]
        wu_free = [[], [], [], []]
        wd_free = [[], [], [], []]
        silup_free = [[], []]
        smax_free = [[], []]
        attnq_free = [[], []]
        xn_stage_free = [[], [], [], []]

        wqkvT_ready = evo[-2:]
        woutT_ready = evo[-2:]
        for s in range(NSEQ):

            def wqkvT(dc, e0, n):
                return A[:, dc * 1536 + e0: dc * 1536 + e0 + n]

            def woutT(eb, o0, n):
                return A[:, 12288 + eb * 1024 + o0: 12288 + eb * 1024 + o0 + n]

            def h_act_ap(hc, c0, n):
                if hc < 8:
                    return k_fm[:, hc * 1024 + c0: hc * 1024 + c0 + n]
                if hc < 16:
                    return v_tm[:, (hc - 8) * 1024 + c0: (hc - 8) * 1024 + c0 + n]
                return A[:, 16384 + (hc - 16) * 1024 + c0: 16384 + (hc - 16) * 1024 + c0 + n]

            xn_ready = {}
            k_ready = {}
            v_ready = {}

            for tc in range(4):
                for tt in range(4):
                    ti = tc * 4 + tt
                    p = ti % 3
                    pp = ti % 4
                    ps_ = ti % 2
                    xs = x_stage[:, p * D:(p + 1) * D]
                    lx = dma("sync", xs, x_in[s, ti * 128:(ti + 1) * 128, :],
                             waits=x_stage_free[p])
                    em.op("scalar", lambda e, xs=xs, pp=ps_: e.activation(
                        junk[:], xs, AF.Identity, scale=1.0 / D,
                        accum_out=stats[:, pp * 8 + 0: pp * 8 + 1]), [lx], inc="SE")
                    em.op("scalar", lambda e, xs=xs, pp=ps_: e.activation(
                        junk[:], xs, AF.Square, scale=1.0 / 32.0,
                        accum_out=stats[:, pp * 8 + 1: pp * 8 + 2]), [], inc="SE")
                    t_sd = em.op("scalar", lambda e: e.drain(), [], inc="SE")
                    em.op("vector", lambda e, pp=ps_: e.tensor_mul(
                        stats[:, pp * 8 + 2: pp * 8 + 3],
                        stats[:, pp * 8 + 0: pp * 8 + 1],
                        stats[:, pp * 8 + 0: pp * 8 + 1]), [t_sd], inc="VE")
                    em.op("vector", lambda e: e.drain(), [], inc="VE")
                    em.op("vector", lambda e, pp=ps_: e.tensor_sub(
                        stats[:, pp * 8 + 3: pp * 8 + 4],
                        stats[:, pp * 8 + 1: pp * 8 + 2],
                        stats[:, pp * 8 + 2: pp * 8 + 3]), [], inc="VE")
                    t_var = em.op("vector", lambda e: e.drain(), [], inc="VE")
                    em.op("scalar", lambda e, pp=ps_: e.activation(
                        stats[:, pp * 8 + 6: pp * 8 + 7],
                        stats[:, pp * 8 + 3: pp * 8 + 4],
                        AF.Sqrt, bias=epsv[:]), [t_var, tok_eps], inc="SE")
                    t_sq = em.op("scalar", lambda e: e.drain(), [], inc="SE")
                    em.op("vector", lambda e, pp=ps_: e.reciprocal(
                        stats[:, pp * 8 + 4: pp * 8 + 5],
                        stats[:, pp * 8 + 6: pp * 8 + 7]), [t_sq], inc="VE")
                    t_rs = em.op("vector", lambda e: e.drain(), [], inc="VE")
                    em.op("vector", lambda e, pp=ps_: e.scalar_tensor_tensor(
                        stats[:, pp * 8 + 5: pp * 8 + 6],
                        stats[:, pp * 8 + 0: pp * 8 + 1], -1.0,
                        stats[:, pp * 8 + 4: pp * 8 + 5],
                        ALU.mult, ALU.mult), [t_rs], inc="VE")
                    t_nb = em.op("vector", lambda e: e.drain(), [], inc="VE")
                    xns = xn_stage[:, pp * D:(pp + 1) * D]
                    t_xn = em.op("scalar", lambda e, xs=xs, xns=xns, pp=ps_: e.activation(
                        xns, xs, AF.Identity,
                        scale=stats[:, pp * 8 + 4: pp * 8 + 5],
                        bias=stats[:, pp * 8 + 5: pp * 8 + 6]),
                        [t_nb] + xn_stage_free[pp], inc="SE")
                    x_stage_free[p] = [t_xn]
                    jobs = []
                    for dc in range(DC):
                        src = xns[:, dc * 128:(dc + 1) * 128]
                        dst = xn_fm[:, dc * T + ti * 128: dc * T + ti * 128 + 128]
                        jobs.append((dst, src, [t_xn]))
                    evs = transpose_batch(jobs)
                    xn_ready[ti] = evs
                    xn_stage_free[pp] = [evs[-1]]

                xw = []
                for ti in range(tc * 4, tc * 4 + 4):
                    xw += xn_ready[ti]
                for kb in range(4):
                    slot = psf_slot()
                    w = xw + wqkvT_ready + [psF_free[slot]]
                    tokm = None
                    for dc in range(DC):
                        tokm = em.op("tensor", lambda e, dc=dc, kb=kb, tc=tc, sl=slot: e.matmul(
                            psF[sl][:, 0:512],
                            wqkvT(dc, 512 + kb * 128, 128),
                            xn_fm[:, dc * T + tc * 512: dc * T + tc * 512 + 512],
                            start=(dc == 0), stop=(dc == DC - 1)),
                            w if dc == 0 else [], inc=None)
                        w = []
                    tokm = pe_fence(psF[slot][:, 0:128])
                    et = evac(k_fm[:, kb * T + tc * 512: kb * T + tc * 512 + 512],
                              psF[slot][:, 0:512], [tokm])
                    psF_free[slot] = et
                    k_ready[(kb, tc)] = et
                for tt in range(4):
                    ti = tc * 4 + tt
                    slot = psf_slot()
                    w = xn_ready[ti] + wqkvT_ready + [psF_free[slot]]
                    tokm = None
                    for dc in range(DC):
                        tokm = em.op("tensor", lambda e, dc=dc, ti=ti, sl=slot: e.matmul(
                            psF[sl][:, 0:512],
                            xn_fm[:, dc * T + ti * 128: dc * T + ti * 128 + 128],
                            wqkvT(dc, 1024, 512),
                            start=(dc == 0), stop=(dc == DC - 1)),
                            w if dc == 0 else [], inc=None)
                        w = []
                    tokm = pe_fence(psF[slot][:, 0:128])
                    et = evac(v_tm[:, ti * E: ti * E + E], psF[slot][:, 0:512], [tokm])
                    psF_free[slot] = et
                    v_ready[ti] = et

            attnT_free = []
            qgrp_free = []
            av_free = []
            prj_done = {}

            def p2_stage12(g):
                nonlocal qgrp_free
                gp = g % 2
                aq = attn_q[:, gp * T:(gp + 1) * T]
                qev = []
                for eb in range(4):
                    slot = psf_slot()
                    w = wqkvT_ready + qgrp_free + [psF_free[slot]]
                    qgrp_free = []
                    tokm = None
                    for dc in range(DC):
                        tokm = em.op("tensor", lambda e, dc=dc, eb=eb, g=g, sl=slot: e.matmul(
                            psF[sl][:, 0:256],
                            wqkvT(dc, eb * 128, 128),
                            xn_fm[:, dc * T + g * 256: dc * T + g * 256 + 256],
                            start=(dc == 0), stop=(dc == DC - 1)),
                            w if dc == 0 else [], inc=None)
                        w = []
                    tokm = pe_fence(psF[slot][:, 0:128])
                    et = evac(q_grp[:, eb * 256:(eb + 1) * 256], psF[slot][:, 0:256], [tokm])
                    psF_free[slot] = et
                    qev.append(et)

                tile_ctx = {}
                for pos in range(2):
                    i = 2 * g + pos
                    L = (i + 1) * 128
                    jc = (L + 511) // 512
                    sc_slots = []
                    for c in range(jc):
                        wdt = 512 if c < jc - 1 else L - 512 * (jc - 1)
                        slot = psf_slot()
                        need_k = [k_ready[(kb, c)] for kb in range(4)]
                        w = qev + need_k + [psF_free[slot]]
                        for qeb in range(4):
                            em.op("tensor",
                                  lambda e, qeb=qeb, c=c, wdt=wdt, pos=pos, sl=slot: e.matmul(
                                      psF[sl][:, 0:wdt],
                                      q_grp[:, qeb * 256 + pos * 128: qeb * 256 + pos * 128 + 128],
                                      k_fm[:, qeb * T + c * 512: qeb * T + c * 512 + wdt],
                                      start=(qeb == 0), stop=(qeb == 3)),
                                  w if qeb == 0 else [], inc=None)
                            w = []
                        sc_slots.append((slot, wdt))
                    t_sc = pe_fence(psF[sc_slots[-1][0]][:, 0:128])
                    lslot, lw = sc_slots[-1]
                    t_mask = em.op("vector", lambda e, sl=lslot, lw=lw: e.tensor_add(
                        psF[sl][:, lw - 128: lw], psF[sl][:, lw - 128: lw], diag[:]),
                        [t_sc], inc="VE")
                    for c, (sl, wdt) in enumerate(sc_slots):
                        em.op("vector", lambda e, c=c, sl=sl, wdt=wdt: e.reduce_max(
                            mxp[:, c: c + 1], psF[sl][:, 0:wdt], axis=AX.X),
                            [t_sc, t_mask] if c == jc - 1 else [t_sc],
                            inc="VE")
                    em.op("vector", lambda e: e.drain(), [], inc="VE")
                    em.op("vector", lambda e, jc=jc, pos=pos: e.reduce_max(
                        smax[:, pos * 2: pos * 2 + 1], mxp[:, 0:jc], axis=AX.X),
                        smax_free[pos], inc="VE")
                    em.op("vector", lambda e: e.drain(), [], inc="VE")
                    em.op("vector", lambda e, pos=pos: e.tensor_scalar_mul(
                        smax[:, pos * 2 + 1: pos * 2 + 2],
                        smax[:, pos * 2: pos * 2 + 1], -SCALE), [], inc="VE")
                    t_negmax = em.op("vector", lambda e: e.drain(), [], inc="VE")
                    for c, (sl, wdt) in enumerate(sc_slots):
                        te = em.op("scalar",
                                   lambda e, c=c, sl=sl, wdt=wdt, pos=pos, aq=aq: e.activation(
                                       aq[:, c * 512: c * 512 + wdt],
                                       psF[sl][:, 0:wdt], AF.Exp,
                                       bias=smax[:, pos * 2 + 1: pos * 2 + 2],
                                       scale=SCALE,
                                       accum_out=rsp[:, c: c + 1]),
                                   [t_negmax] + (attnq_free[gp] if c == 0 else []),
                                   inc="SE")
                        psF_free[sl] = te
                    t_se = em.op("scalar", lambda e: e.drain(), [], inc="SE")
                    smax_free[pos] = [t_se]
                    em.op("vector", lambda e, jc=jc, pos=pos: e.reduce_sum(
                        srec[:, pos * 2: pos * 2 + 1], rsp[:, 0:jc], axis=AX.X),
                        [t_se], inc="VE")
                    em.op("vector", lambda e: e.drain(), [], inc="VE")
                    em.op("vector", lambda e, pos=pos: e.reciprocal(
                        srec[:, pos * 2 + 1: pos * 2 + 2],
                        srec[:, pos * 2: pos * 2 + 1]), [], inc="VE")
                    em.op("vector", lambda e: e.drain(), [], inc="VE")
                    t_norm = em.op("vector", lambda e, L=L, pos=pos, aq=aq: e.tensor_scalar_mul(
                        aq[:, 0:L], aq[:, 0:L],
                        srec[:, pos * 2 + 1: pos * 2 + 2]), [], inc="VE")
                    tile_ctx[pos] = t_norm
                return {"g": g, "qev": qev, "t_norm": tile_ctx, "aq": aq}

            def p2_stage45(ctx):
                nonlocal attnT_free, av_free, qgrp_free
                g = ctx["g"]
                gp = g % 2
                aq = ctx["aq"]
                tile_T_evs = {}
                for pos in range(2):
                    i = 2 * g + pos
                    jobs = []
                    for jb in range(i + 1):
                        dst = attnT_g[:, jb * 256 + pos * 128: jb * 256 + pos * 128 + 128]
                        jobs.append((dst, aq[:, jb * 128:(jb + 1) * 128],
                                     [ctx["t_norm"][pos]] + (attnT_free if jb == 0 else [])))
                    evs = transpose_batch(jobs)
                    attnT_free = []
                    tile_T_evs[pos] = evs
                attnq_free[gp] = tile_T_evs[0] + tile_T_evs[1]

                t_zero = em.op("gpsimd", lambda e, g=g: e.memset(
                    attnT_g[:, (2 * g + 1) * 256: (2 * g + 1) * 256 + 128], 0.0),
                    [ctx["qev"][0]], inc="GE")

                all_T = tile_T_evs[0] + tile_T_evs[1] + [t_zero]
                nj = 2 * g + 2
                av_ev = []
                av_slots = []
                for eb in range(4):
                    slot = psf_slot()
                    w = all_T + av_free + [v_ready[jb] for jb in range(nj)] + [psF_free[slot]]
                    av_free = []
                    for jb in range(nj):
                        em.op("tensor", lambda e, jb=jb, eb=eb, sl=slot, nj=nj: e.matmul(
                            psF[sl][:, 0:256],
                            v_tm[:, jb * E + eb * 128: jb * E + eb * 128 + 128],
                            attnT_g[:, jb * 256: jb * 256 + 256],
                            start=(jb == 0), stop=(jb == nj - 1)),
                            w if jb == 0 else [], inc=None)
                        w = []
                    av_slots.append(slot)
                t_av = pe_fence(psF[av_slots[-1]][:, 0:128])
                for eb, slot in enumerate(av_slots):
                    et = evac(av_fm[:, eb * 256:(eb + 1) * 256], psF[slot][:, 0:256], [t_av])
                    psF_free[slot] = et
                    av_ev.append(et)

                for pos in range(2):
                    ti = 2 * g + pos
                    p = pos
                    lxa = dma("sync", xa_stage[:, p * D:(p + 1) * D],
                              x_in[s, ti * 128:(ti + 1) * 128, :],
                              waits=xa_stage_free[p])
                    prj_toks = []
                    for oh in range(2):
                        slot = psf_slot()
                        w = av_ev + woutT_ready + [tok_ones, tok_brow, psF_free[slot]]
                        for eb in range(4):
                            em.op("tensor", lambda e, eb=eb, oh=oh, pos=pos, sl=slot: e.matmul(
                                psF[sl][:, 0:512],
                                av_fm[:, eb * 256 + pos * 128: eb * 256 + pos * 128 + 128],
                                woutT(eb, oh * 512, 512),
                                start=(eb == 0), stop=False),
                                w if eb == 0 else [])
                            w = []
                        em.op("tensor", lambda e, oh=oh, sl=slot: e.matmul(
                            psF[sl][:, 0:512], ones_bf[:],
                            b_row_bf[:, oh * 512: oh * 512 + 512],
                            start=False, stop=True), [], inc=None)
                        prj_toks.append((slot, None))
                    t_prj = pe_fence(psF[prj_toks[-1][0]][:, 0:128])
                    prj_toks = [(sl, t_prj) for (sl, _unused) in prj_toks]
                    vts = []
                    for oh, (slot, tokb) in enumerate(prj_toks):
                        vt = em.op("vector", lambda e, oh=oh, sl=slot, p=p: e.tensor_add(
                            xa_stage[:, p * D + oh * 512: p * D + oh * 512 + 512],
                            psF[sl][:, 0:512],
                            xa_stage[:, p * D + oh * 512: p * D + oh * 512 + 512]),
                            [lxa, tokb], inc="VE")
                        psF_free[slot] = vt
                        vts.append(vt)
                    sxa = dma("gpsimd", scr_xa[ti], xa_stage[:, p * D:(p + 1) * D],
                              waits=vts, sem="ST")
                    xa_stage_free[p] = [sxa]
                    prj_done[ti] = sxa
                av_free = av_ev
                qgrp_free = ctx["qev"]

            prev_ctx = None
            for g in range(8):
                ctx = p2_stage12(g)
                if prev_ctx is not None:
                    p2_stage45(prev_ctx)
                prev_ctx = ctx
            p2_stage45(prev_ctx)

            last_store = None
            prev_th = []
            for hf in range(2):
                ha_ev = {}
                pair_i = [0]
                for hb in range(HB):
                    p = hb % 4
                    lg = dma("sync", wg_buf[:, p * 1024:(p + 1) * 1024], scr_g[hb],
                             waits=wg_free[p])
                    lu = dma("sync", wu_buf[:, p * 1024:(p + 1) * 1024], scr_u[hb],
                             waits=wu_free[p])
                    pa = pair_i[0] % 3
                    pb = (pair_i[0] + 1) % 3
                    pair_i[0] += 2
                    wgw = [lg, psF_free[2 * pa], psF_free[2 * pa + 1]]
                    for dc in range(DC):
                        for th in range(2):
                            t0 = hf * 1024 + th * 512
                            em.op("tensor", lambda e, dc=dc, p=p, t0=t0, pa=pa, th=th: e.matmul(
                                psF[2 * pa + th][:, 0:512],
                                wg_buf[:, p * 1024 + dc * 128: p * 1024 + dc * 128 + 128],
                                xn_fm[:, dc * T + t0: dc * T + t0 + 512],
                                start=(dc == 0), stop=(dc == DC - 1)),
                                wgw, inc=None)
                            wgw = []
                    wuw = [lu, psF_free[2 * pb], psF_free[2 * pb + 1]]
                    for dc in range(DC):
                        for th in range(2):
                            t0 = hf * 1024 + th * 512
                            em.op("tensor", lambda e, dc=dc, p=p, t0=t0, pb=pb, th=th: e.matmul(
                                psF[2 * pb + th][:, 0:512],
                                wu_buf[:, p * 1024 + dc * 128: p * 1024 + dc * 128 + 128],
                                xn_fm[:, dc * T + t0: dc * T + t0 + 512],
                                start=(dc == 0), stop=(dc == DC - 1)),
                                wuw, inc=None)
                            wuw = []
                    tokf = pe_fence(psF[2 * pb + 1][:, 0:128])
                    t_sil = em.op("scalar", lambda e, pa=pa: e.activation(
                        sil_buf[:, 0:1024], psB3[pa][:, 0:1024],
                        AF.Silu, scale=DEQ), [tokf] + silup_free[0], inc="SE")
                    psF_free[2 * pa] = t_sil
                    psF_free[2 * pa + 1] = t_sil
                    t_h = em.op("vector", lambda e, hb=hb, pb=pb: e.scalar_tensor_tensor(
                        h_act_ap(hb, 0, 1024),
                        psB3[pb][:, 0:1024], DEQ,
                        sil_buf[:, 0:1024],
                        ALU.mult, ALU.mult),
                        [t_sil] + (arena_users if hb == 0 else []),
                        inc="VE")
                    psF_free[2 * pb] = t_h
                    psF_free[2 * pb + 1] = t_h
                    arena_users = []
                    silup_free[0] = [t_h]
                    ha_ev[(hb, 0)] = t_h
                    ha_ev[(hb, 1)] = t_h
                    wg_free[p] = [t_h]
                    wu_free[p] = [t_h]

                ha_all = [ha_ev[(hb, tc2)] for hb in range(HB) for tc2 in range(2)]
                for tbs in ([0, 1, 2], [3, 4, 5], [6, 7]):
                    mm_toks = {}
                    for hc in range(HB):
                        p = hc % 4
                        ld = dma("sync", wd_buf[:, p * 1024:(p + 1) * 1024], scr_d[hc],
                                 waits=wd_free[p])
                        chunk_last = None
                        for k, tb in enumerate(tbs):
                            for oh in range(2):
                                sl = k * 2 + oh
                                w = [ld] if (k == 0 and oh == 0) else []
                                if hc == 0:
                                    w = w + ha_all + [psF_free[sl]]
                                is_last_mm = (k == len(tbs) - 1 and oh == 1)
                                tokm = em.op("tensor",
                                             lambda e, hc=hc, tb=tb, oh=oh, hf=hf, p=p, sl=sl: e.matmul(
                                                 psF[sl][:, 0:512],
                                                 h_act_ap(hc, tb * 128, 128),
                                                 wd_buf[:, p * 1024 + oh * 512: p * 1024 + oh * 512 + 512],
                                                 start=(hc == 0), stop=(hc == HB - 1)),
                                             w, inc="PE" if (is_last_mm or hc == HB - 1) else None)
                                if hc == HB - 1:
                                    mm_toks[(tb, oh)] = tokm
                                if is_last_mm:
                                    chunk_last = tokm
                        wd_free[p] = [chunk_last]
                    t_dn = pe_fence(psF[2 * len(tbs) - 1][:, 0:128])
                    for kk in mm_toks:
                        mm_toks[kk] = t_dn
                    for k, tb in enumerate(tbs):
                        ti = hf * 8 + tb
                        p = ti % 2
                        lxa = dma("sync", xa_stage[:, p * D:(p + 1) * D], scr_xa[ti],
                                  waits=xa_stage_free[p] + [prj_done[ti]])
                        vts = []
                        for oh in range(2):
                            sl = k * 2 + oh
                            vt = em.op("vector", lambda e, sl=sl, oh=oh, p=p: e.scalar_tensor_tensor(
                                out_stage[:, p * D + oh * 512: p * D + oh * 512 + 512],
                                psF[sl][:, 0:512], DEQ,
                                xa_stage[:, p * D + oh * 512: p * D + oh * 512 + 512],
                                ALU.mult, ALU.add),
                                [lxa, mm_toks[(tb, oh)]] + out_stage_free[p], inc="VE")
                            out_stage_free[p] = []
                            psF_free[sl] = vt
                            vts.append(vt)
                        so = dma("gpsimd", out[s, ti * 128:(ti + 1) * 128, :],
                                 out_stage[:, p * D:(p + 1) * D], waits=vts, sem="ST")
                        out_stage_free[p] = [so]
                        xa_stage_free[p] = [so]
                        last_store = so
            arena_users = [last_store]

        for i in range(N_LANES["ST"]):
            tot = em.n(f"ST{i}")
            em.op("gpsimd", lambda e, i=i, tot=tot: e.wait_ge(sems[f"ST{i}"], tot))

        with nc.Block() as block:
            def go_factory(eng_name):
                is_pe = eng_name == "tensor"
                defer_drain = eng_name in ("vector", "scalar", "gpsimd")
                def go(eng):
                    for waits, fn, tok, kind in em.ops[eng_name]:
                        for sname, val in waits:
                            eng.wait_ge(sems[sname], val)
                        inst = fn(eng)
                        if tok is not None:
                            inst.then_inc(sems[tok[0]], tok[1])
                return go
            block.sync(go_factory("sync"))
            block.scalar(go_factory("scalar"))
            block.vector(go_factory("vector"))
            block.tensor(go_factory("tensor"))
            block.gpsimd(go_factory("gpsimd"))
    return nc


_NC_CACHE = None


def kernel(L0_attn_x, L0_attn_w_qkv, L0_attn_w_out, L0_attn_b_out,
           L0_ffn_w_gate, L0_ffn_w_up, L0_ffn_w_down):
    global _NC_CACHE
    x = np.ascontiguousarray(L0_attn_x, dtype=np.float32)
    wqkv = np.ascontiguousarray(L0_attn_w_qkv, dtype=np.float32)
    wout = np.ascontiguousarray(L0_attn_w_out, dtype=np.float32)
    bout = np.ascontiguousarray(L0_attn_b_out, dtype=np.float32).reshape(1, D)
    wg = np.ascontiguousarray(L0_ffn_w_gate, dtype=np.int32)
    wu = np.ascontiguousarray(L0_ffn_w_up, dtype=np.int32)
    wd = np.ascontiguousarray(L0_ffn_w_down, dtype=np.int32)

    if _NC_CACHE is None:
        _NC_CACHE = build_nc()
    nc = _NC_CACHE
    in_maps = []
    for c in range(8):
        in_maps.append({
            "x_in": np.ascontiguousarray(x[4 * c: 4 * (c + 1)]),
            "w_qkv": wqkv, "w_out": wout, "b_out": bout,
            "w_gate": wg, "w_up": wu, "w_down": wd,
        })
    res = run_bass_kernel_spmd(nc, in_maps, core_ids=list(range(8)))
    return np.concatenate([r["out"] for r in res.results], axis=0)


# version 19
# speedup vs baseline: 1.0331x; 1.0331x over previous
import numpy as np
from contextlib import ExitStack

import concourse.bass as bass
import concourse.mybir as mybir
from concourse.bass_utils import run_bass_kernel_spmd

F32 = mybir.dt.float32
BF16 = mybir.dt.bfloat16
I32 = mybir.dt.int32
AF = mybir.ActivationFunctionType
ALU = mybir.AluOpType
AX = mybir.AxisListType

NSEQ = 4
T = 2048
D = 1024
E = 512
H = 3072
NT = 16
DC = 8
HB = 24
SCALE = float(E) ** -0.5
EPS = 1e-5
MASKVAL = -1.0e30
DEQ = 1.0 / 127.0

ENGINES = ("sync", "scalar", "vector", "tensor", "gpsimd")
DMA_SEMS = ("LD", "GL", "ST")


class Em:

    def __init__(self):
        self.ops = {e: [] for e in ENGINES}
        self.cnt = {}

    def n(self, sem):
        return self.cnt.get(sem, 0)

    def op(self, eng, fn, waits=(), inc=None, amt=1, kind="op"):
        w = {}
        for t in waits:
            if t is None:
                continue
            s, v = t
            assert v <= self.cnt.get(s, 0), f"wait on future token {s}={v}"
            w[s] = max(w.get(s, 0), v)
        tok = None
        if inc is not None:
            self.cnt[inc] = self.cnt.get(inc, 0) + amt
            tok = (inc, amt)
            ret = (inc, self.cnt[inc])
        else:
            ret = None
        self.ops[eng].append((sorted(w.items()), fn, tok, kind))
        return ret


def build_nc():
    nc = bass.Bass("TRN2", target_bir_lowering=False)

    x_in = nc.dram_tensor("x_in", [NSEQ, T, D], F32, kind="ExternalInput").ap()
    w_qkv = nc.dram_tensor("w_qkv", [3 * E, D], F32, kind="ExternalInput").ap()
    w_out = nc.dram_tensor("w_out", [D, E], F32, kind="ExternalInput").ap()
    b_out = nc.dram_tensor("b_out", [1, D], F32, kind="ExternalInput").ap()
    w_gate = nc.dram_tensor("w_gate", [H, D], I32, kind="ExternalInput").ap()
    w_up = nc.dram_tensor("w_up", [H, D], I32, kind="ExternalInput").ap()
    w_down = nc.dram_tensor("w_down", [D, H], I32, kind="ExternalInput").ap()
    out = nc.dram_tensor("out", [NSEQ, T, D], F32, kind="ExternalOutput").ap()

    scr_g = nc.dram_tensor("scr_g", [HB, 128, DC * 128], BF16, kind="Internal").ap()
    scr_u = nc.dram_tensor("scr_u", [HB, 128, DC * 128], BF16, kind="Internal").ap()
    scr_d = nc.dram_tensor("scr_d", [HB, 128, D], BF16, kind="Internal").ap()
    scr_xa = nc.dram_tensor("scr_xa", [NT, 128, D], F32, kind="Internal").ap()

    es = ExitStack()
    with es:
        sb = lambda *a, **k: es.enter_context(nc.sbuf_tensor(*a, **k))
        A = sb("arena", [128, HB * 1024], BF16)
        xn_fm = sb("xn_fm", [128, DC * T], BF16)
        k_fm = sb("k_fm", [128, 4 * T], BF16)
        v_tm = sb("v_tm", [128, NT * E], BF16)
        q_grp = sb("q_grp", [128, 4 * 256], BF16)
        attn_q = sb("attn_q", [128, 2 * T], BF16)
        attnT_g = sb("attnT_g", [128, NT * 256], BF16)
        av_fm = sb("av_fm", [128, 4 * 256], BF16)
        xn_stage = sb("xn_stage", [128, 4 * D], BF16)
        wg_buf = sb("wg_buf", [128, 4 * 1024], BF16)
        wu_buf = sb("wu_buf", [128, 4 * 1024], BF16)
        wd_buf = sb("wd_buf", [128, 4 * 1024], BF16)
        sil_buf = sb("sil_buf", [128, 2 * 512], BF16)
        ident = sb("ident", [128, 128], BF16)
        zeros_bf = sb("zeros_bf", [128, 128], BF16)
        ones_bf = sb("ones_bf", [1, 128], BF16)
        b_row_bf = sb("b_row_bf", [1, D], BF16)

        x_stage = sb("x_stage", [128, 4 * D], F32)
        xa_stage = sb("xa_stage", [128, 2 * D], F32)
        out_stage = sb("out_stage", [128, 2 * D], F32)
        diag = sb("diag", [128, 128], F32)
        b_row_f = sb("b_row_f", [1, D], F32)
        stats = sb("stats", [128, 24], F32)
        mxp = sb("mxp", [128, 8], F32)
        rsp = sb("rsp", [128, 8], F32)
        smax = sb("smax", [128, 4], F32)
        srec = sb("srec", [128, 16], F32)
        rs_bf = sb("rs_bf", [128, 4], BF16)
        rs_row = sb("rs_row", [1, 4 * 128], BF16)
        epsv = sb("epsv", [128, 1], F32)

        psB3 = [es.enter_context(nc.psum_tensor(f"psB{i}", [128, 1024], F32)) for i in range(3)]
        psF = [psB3[i // 2][:, (i % 2) * 512:(i % 2) * 512 + 512] for i in range(6)]
        psT = [es.enter_context(nc.psum_tensor(f"psT{i}", [128, 1024], BF16)) for i in range(2)]

        sems = {}
        for name in ("PE", "VE", "SE", "GE"):
            sems[name] = es.enter_context(nc.semaphore(name))
        N_LANES = {"LD": 8, "ST": 4, "GL": 2}
        for base, n in N_LANES.items():
            for i in range(n):
                sems[f"{base}{i}"] = es.enter_context(nc.semaphore(f"{base}{i}"))
        lane_rr = {"LD": 0, "ST": 0, "GL": 0}

        em = Em()

        def dma(queue, dst, src, waits=(), sem=None):
            if sem is None:
                sem = "ST" if queue == "gpsimd" else "LD"
            lane = lane_rr[sem]
            lane_rr[sem] = (lane + 1) % N_LANES[sem]
            return em.op(queue, lambda e, d=dst, s=src: e.dma_start(out=d, in_=s),
                         waits, inc=f"{sem}{lane}", amt=16, kind="dma")

        evac_toggle = [0]

        def evac(dst, src, waits, prefer=None):
            which = prefer
            if which is None:
                which = "vector" if evac_toggle[0] == 0 else "scalar"
                evac_toggle[0] ^= 1
            if which == "vector":
                return em.op("vector", lambda e, d=dst, s=src: e.tensor_copy(d, s),
                             waits, inc="VE")
            return em.op("scalar", lambda e, d=dst, s=src: e.copy(d, s),
                         waits, inc="SE")

        fence_first = [True]

        def pe_fence(bank_ap):
            w = [tok_zeros, tok_masks] if fence_first[0] else []
            fence_first[0] = False
            return em.op("tensor", lambda e, b=bank_ap: e.matmul(
                b, zeros_bf[:], ident[:],
                start=False, stop=True, skip_group_check=True), w, inc="PE")

        psT_free = [None, None]
        psT_i = [0]

        def transpose_batch(jobs):
            toks = []
            for i0 in range(0, len(jobs), 2):
                pair = jobs[i0:i0 + 2]
                slots = []
                for dst, src, w in pair:
                    slot = psT_i[0] % 2
                    psT_i[0] += 1
                    ww = list(w) + [psT_free[slot]]
                    pr, fr = src.shape[0], src.shape[1]
                    em.op("tensor",
                          lambda e, s=src, p=psT[slot], pr=pr, fr=fr: e.transpose(
                              p[0:fr, 0:pr], s, ident[:]),
                          ww, inc="PE")
                    slots.append(slot)
                t_dr = em.op("tensor", lambda e: e.drain(), [], inc="PE")
                for (dst, src, w), slot in zip(pair, slots):
                    pr, fr = src.shape[0], src.shape[1]
                    et = evac(dst, psT[slot][0:fr, 0:pr], [t_dr])
                    psT_free[slot] = et
                    toks.append(et)
            return toks

        def transpose_to(dst, src_ap, waits):
            return transpose_batch([(dst, src_ap, waits)])[0]

        psF_free = [None] * 6
        psf_i = [0]

        def psf_slot():
            slot = psf_i[0] % 6
            psf_i[0] += 1
            return slot

        em.op("gpsimd", lambda e: e.memset(ident[:], 0.0), inc="GE")
        em.op("gpsimd", lambda e: e.affine_select(
            out=ident[:], in_=ident[:], compare_op=ALU.not_equal, fill=1.0,
            base=0, pattern=[[-1, 128]], channel_multiplier=1), inc="GE")
        em.op("gpsimd", lambda e: e.memset(diag[:], 0.0), inc="GE")
        tok_masks = em.op("gpsimd", lambda e: e.affine_select(
            out=diag[:], in_=diag[:], compare_op=ALU.is_ge, fill=MASKVAL,
            base=0, pattern=[[-1, 128]], channel_multiplier=1), inc="GE")
        tok_ones = em.op("vector", lambda e: e.memset(ones_bf[:], 1.0), inc="VE")
        tok_zeros = em.op("vector", lambda e: e.memset(zeros_bf[:], 0.0), inc="VE")
        tok_eps = em.op("vector", lambda e: e.memset(epsv[:], EPS), inc="VE")

        tok = dma("sync", b_row_f[:], b_out)
        tok_brow = em.op("vector", lambda e: e.tensor_copy(b_row_bf[:], b_row_f[:]),
                         [tok], inc="VE")

        ldd = []
        for ob in range(8):
            ldd.append(dma("gpsimd", A[:, ob * 3072:(ob + 1) * 3072],
                           w_down[ob * 128:(ob + 1) * 128, :], waits=[tok_masks] if ob == 0 else [], sem="GL"))
        xnfree = [[], [], [], []]
        for hc in range(HB):
            p = hc % 4
            jobs = []
            for ob in range(8):
                src = A[:, ob * 3072 + hc * 128: ob * 3072 + hc * 128 + 128]
                dst = k_fm[:, p * 1024 + ob * 128: p * 1024 + ob * 128 + 128]
                jobs.append((dst, src, [ldd[ob]] + xnfree[p]))
            evs = transpose_batch(jobs)
            st = dma("sync", scr_d[hc], k_fm[:, p * 1024:(p + 1) * 1024],
                     waits=evs, sem="ST")
            xnfree[p] = [st]
        wd_done = sum(xnfree, [])

        def nat12(blk):
            if blk < 8:
                return k_fm[:, blk * 1024:(blk + 1) * 1024]
            return v_tm[:, (blk - 8) * 1024:(blk - 7) * 1024]
        ldtoks = []
        for blk in range(12):
            ldtoks.append(dma("gpsimd", nat12(blk),
                              w_qkv[blk * 128:(blk + 1) * 128, :],
                              waits=wd_done if blk < 4 else [], sem="GL"))
        ev = []
        for eb in range(12):
            jobs = []
            for dc in range(DC):
                src = nat12(eb)[:, dc * 128: dc * 128 + 128]
                dst = A[:, dc * 1536 + eb * 128: dc * 1536 + eb * 128 + 128]
                jobs.append((dst, src, [ldtoks[eb]]))
            ev += transpose_batch(jobs)
        w1 = [ev[-1]]
        ldo = []
        for ob in range(8):
            ldo.append(dma("gpsimd", v_tm[:, 4096 + ob * 512: 4096 + (ob + 1) * 512],
                           w_out[ob * 128:(ob + 1) * 128, :], waits=w1 + ev, sem="GL"))
            w1 = []
            ev = []
        evo = []
        for eb in range(4):
            jobs = []
            for ob in range(8):
                src = v_tm[:, 4096 + ob * 512 + eb * 128: 4096 + ob * 512 + eb * 128 + 128]
                dst = A[:, 12288 + eb * 1024 + ob * 128: 12288 + eb * 1024 + ob * 128 + 128]
                jobs.append((dst, src, [ldo[ob]]))
            evo += transpose_batch(jobs)
        natfree = [list(wd_done) for _ in range(4)]
        tfree = [[] for _ in range(4)]
        prep_gu_state = {"natfree": natfree, "tfree": tfree, "stores": []}

        def prep_gu_step(idx):
            wsrc, wscr = ((w_gate, scr_g), (w_up, scr_u))[idx // HB]
            hb = idx % HB
            p = hb % 4
            nf, tf = prep_gu_state["natfree"], prep_gu_state["tfree"]
            lt = dma("gpsimd", A[:, 16384 + p * 1024: 16384 + (p + 1) * 1024],
                     wsrc[hb * 128:(hb + 1) * 128, :], waits=nf[p], sem="GL")
            jobs = []
            for dc in range(DC):
                src = A[:, 16384 + p * 1024 + dc * 128: 16384 + p * 1024 + dc * 128 + 128]
                dst = A[:, 20480 + p * 1024 + dc * 128: 20480 + p * 1024 + dc * 128 + 128]
                jobs.append((dst, src, [lt] + tf[p]))
            evs = transpose_batch(jobs)
            st = dma("sync", wscr[hb], A[:, 20480 + p * 1024: 20480 + (p + 1) * 1024],
                     waits=evs, sem="ST")
            nf[p] = evs
            tf[p] = [st]
            prep_gu_state["stores"].append(st)

        arena_users = []
        x_stage_free = [[], [], [], []]
        xa_stage_free = [[], []]
        out_stage_free = [[], []]
        wg_free = [[], [], [], []]
        wu_free = [[], [], [], []]
        wd_free = [[], [], [], []]
        silup_free = [[], []]
        smax_free = [[], []]
        attnq_free = [[], []]
        xn_stage_free = [[], [], [], []]

        wqkvT_ready = evo[-2:]
        woutT_ready = evo[-2:]
        for s in range(NSEQ):

            def wqkvT(dc, e0, n):
                return A[:, dc * 1536 + e0: dc * 1536 + e0 + n]

            def woutT(eb, o0, n):
                return A[:, 12288 + eb * 1024 + o0: 12288 + eb * 1024 + o0 + n]

            def h_act_ap(hc, c0, n):
                if hc < 8:
                    return k_fm[:, hc * 1024 + c0: hc * 1024 + c0 + n]
                if hc < 16:
                    return v_tm[:, (hc - 8) * 1024 + c0: (hc - 8) * 1024 + c0 + n]
                return A[:, 16384 + (hc - 16) * 1024 + c0: 16384 + (hc - 16) * 1024 + c0 + n]

            xn_ready = {}
            k_ready = {}
            v_ready = {}

            for tc in range(4):
                t_lx = {}
                for tt in range(4):
                    ti = tc * 4 + tt
                    p = ti % 4
                    xs = x_stage[:, p * D:(p + 1) * D]
                    lx = dma("sync", xs, x_in[s, ti * 128:(ti + 1) * 128, :],
                             waits=x_stage_free[p])
                    t_lx[tt] = (lx, xs, p)
                for tt in range(4):
                    lx, xs, p = t_lx[tt]
                    em.op("scalar", lambda e, xs=xs, tt=tt, p=p: e.activation(
                        xn_stage[:, p * D:(p + 1) * D], xs, AF.Identity, scale=1.0 / D,
                        accum_out=stats[:, tt: tt + 1]),
                        [lx] + xn_stage_free[p], inc="SE")
                    em.op("scalar", lambda e, xs=xs, tt=tt, p=p: e.activation(
                        xn_stage[:, p * D:(p + 1) * D], xs, AF.Square, scale=1.0 / 32.0,
                        accum_out=stats[:, 4 + tt: 5 + tt]), [], inc="SE")
                t_sd = em.op("scalar", lambda e: e.drain(), [], inc="SE")
                em.op("vector", lambda e: e.scalar_tensor_tensor(
                    stats[:, 8:12], stats[:, 0:4], -1.0, stats[:, 0:4],
                    ALU.mult, ALU.mult), [t_sd], inc="VE")
                em.op("vector", lambda e: e.drain(), [], inc="VE")
                em.op("vector", lambda e: e.tensor_add(
                    stats[:, 12:16], stats[:, 4:8], stats[:, 8:12]), [], inc="VE")
                em.op("vector", lambda e: e.drain(), [], inc="VE")
                em.op("vector", lambda e: e.tensor_scalar_add(
                    stats[:, 12:16], stats[:, 12:16], EPS), [], inc="VE")
                t_var = em.op("vector", lambda e: e.drain(), [], inc="VE")
                em.op("scalar", lambda e: e.activation(
                    stats[:, 12:16], stats[:, 12:16], AF.Sqrt), [t_var], inc="SE")
                t_sq = em.op("scalar", lambda e: e.drain(), [], inc="SE")
                em.op("vector", lambda e: e.reciprocal(
                    stats[:, 16:20], stats[:, 12:16]), [t_sq], inc="VE")
                em.op("vector", lambda e: e.drain(), [], inc="VE")
                em.op("vector", lambda e: e.scalar_tensor_tensor(
                    stats[:, 20:24], stats[:, 0:4], -1.0, stats[:, 16:20],
                    ALU.mult, ALU.mult), [], inc="VE")
                t_nb = em.op("vector", lambda e: e.drain(), [], inc="VE")
                for tt in range(4):
                    ti = tc * 4 + tt
                    lx, xs, p = t_lx[tt]
                    xns = xn_stage[:, p * D:(p + 1) * D]
                    t_xn = em.op("scalar", lambda e, xs=xs, xns=xns, tt=tt: e.activation(
                        xns, xs, AF.Identity,
                        scale=stats[:, 16 + tt: 17 + tt],
                        bias=stats[:, 20 + tt: 21 + tt]), [t_nb], inc="SE")
                    x_stage_free[p] = [t_xn]
                    jobs = []
                    for dc in range(DC):
                        src = xns[:, dc * 128:(dc + 1) * 128]
                        dst = xn_fm[:, dc * T + ti * 128: dc * T + ti * 128 + 128]
                        jobs.append((dst, src, [t_xn]))
                    evs = transpose_batch(jobs)
                    xn_ready[ti] = evs
                    xn_stage_free[p] = [evs[-1]]
                xw = []
                for ti in range(tc * 4, tc * 4 + 4):
                    xw += xn_ready[ti]
                for kb in range(4):
                    slot = psf_slot()
                    w = xw + wqkvT_ready + [psF_free[slot]]
                    tokm = None
                    for dc in range(DC):
                        tokm = em.op("tensor", lambda e, dc=dc, kb=kb, tc=tc, sl=slot: e.matmul(
                            psF[sl][:, 0:512],
                            wqkvT(dc, 512 + kb * 128, 128),
                            xn_fm[:, dc * T + tc * 512: dc * T + tc * 512 + 512],
                            start=(dc == 0), stop=(dc == DC - 1)),
                            w if dc == 0 else [], inc=None)
                        w = []
                    tokm = pe_fence(psF[slot][:, 0:128])
                    et = evac(k_fm[:, kb * T + tc * 512: kb * T + tc * 512 + 512],
                              psF[slot][:, 0:512], [tokm])
                    psF_free[slot] = et
                    k_ready[(kb, tc)] = et
                for tt in range(4):
                    ti = tc * 4 + tt
                    slot = psf_slot()
                    w = xn_ready[ti] + wqkvT_ready + [psF_free[slot]]
                    tokm = None
                    for dc in range(DC):
                        tokm = em.op("tensor", lambda e, dc=dc, ti=ti, sl=slot: e.matmul(
                            psF[sl][:, 0:512],
                            xn_fm[:, dc * T + ti * 128: dc * T + ti * 128 + 128],
                            wqkvT(dc, 1024, 512),
                            start=(dc == 0), stop=(dc == DC - 1)),
                            w if dc == 0 else [], inc=None)
                        w = []
                    tokm = pe_fence(psF[slot][:, 0:128])
                    et = evac(v_tm[:, ti * E: ti * E + E], psF[slot][:, 0:512], [tokm])
                    psF_free[slot] = et
                    v_ready[ti] = et
                if s == 0:
                    for k_ in range(12):
                        prep_gu_step(tc * 12 + k_)
                    if tc == 3:
                        arena_users.extend(prep_gu_state["natfree"][0]
                                           + prep_gu_state["natfree"][1]
                                           + prep_gu_state["natfree"][2]
                                           + prep_gu_state["natfree"][3]
                                           + sum(prep_gu_state["tfree"], []))

            attnT_free = []
            qgrp_free = []
            av_free = []
            prj_done = {}
            srec_free = [[], [], [], []]

            def p2_stage12(g):
                nonlocal qgrp_free
                gp = g % 2
                aq = attn_q[:, gp * T:(gp + 1) * T]
                qev = []
                for eb in range(4):
                    slot = psf_slot()
                    w = wqkvT_ready + qgrp_free + [psF_free[slot]]
                    qgrp_free = []
                    tokm = None
                    for dc in range(DC):
                        tokm = em.op("tensor", lambda e, dc=dc, eb=eb, g=g, sl=slot: e.matmul(
                            psF[sl][:, 0:256],
                            wqkvT(dc, eb * 128, 128),
                            xn_fm[:, dc * T + g * 256: dc * T + g * 256 + 256],
                            start=(dc == 0), stop=(dc == DC - 1)),
                            w if dc == 0 else [], inc=None)
                        w = []
                    tokm = pe_fence(psF[slot][:, 0:128])
                    et = evac(q_grp[:, eb * 256:(eb + 1) * 256], psF[slot][:, 0:256], [tokm])
                    psF_free[slot] = et
                    qev.append(et)

                tile_ctx = {}
                for pos in range(2):
                    i = 2 * g + pos
                    L = (i + 1) * 128
                    jc = (L + 511) // 512
                    sc_slots = []
                    for c in range(jc):
                        wdt = 512 if c < jc - 1 else L - 512 * (jc - 1)
                        slot = psf_slot()
                        need_k = [k_ready[(kb, c)] for kb in range(4)]
                        w = qev + need_k + [psF_free[slot]]
                        for qeb in range(4):
                            em.op("tensor",
                                  lambda e, qeb=qeb, c=c, wdt=wdt, pos=pos, sl=slot: e.matmul(
                                      psF[sl][:, 0:wdt],
                                      q_grp[:, qeb * 256 + pos * 128: qeb * 256 + pos * 128 + 128],
                                      k_fm[:, qeb * T + c * 512: qeb * T + c * 512 + wdt],
                                      start=(qeb == 0), stop=(qeb == 3)),
                                  w if qeb == 0 else [], inc=None)
                            w = []
                        sc_slots.append((slot, wdt))
                    t_sc = pe_fence(psF[sc_slots[-1][0]][:, 0:128])
                    lslot, lw = sc_slots[-1]
                    t_mask = em.op("vector", lambda e, sl=lslot, lw=lw: e.tensor_add(
                        psF[sl][:, lw - 128: lw], psF[sl][:, lw - 128: lw], diag[:]),
                        [t_sc], inc="VE")
                    for c, (sl, wdt) in enumerate(sc_slots):
                        em.op("vector", lambda e, c=c, sl=sl, wdt=wdt: e.reduce_max(
                            mxp[:, c: c + 1], psF[sl][:, 0:wdt], axis=AX.X),
                            [t_sc, t_mask] if c == jc - 1 else [t_sc],
                            inc="VE")
                    em.op("vector", lambda e: e.drain(), [], inc="VE")
                    em.op("vector", lambda e, jc=jc, pos=pos: e.reduce_max(
                        smax[:, pos * 2: pos * 2 + 1], mxp[:, 0:jc], axis=AX.X),
                        smax_free[pos], inc="VE")
                    em.op("vector", lambda e: e.drain(), [], inc="VE")
                    em.op("vector", lambda e, pos=pos: e.tensor_scalar_mul(
                        smax[:, pos * 2 + 1: pos * 2 + 2],
                        smax[:, pos * 2: pos * 2 + 1], -SCALE), [], inc="VE")
                    t_negmax = em.op("vector", lambda e: e.drain(), [], inc="VE")
                    for c, (sl, wdt) in enumerate(sc_slots):
                        te = em.op("scalar",
                                   lambda e, c=c, sl=sl, wdt=wdt, pos=pos, aq=aq: e.activation(
                                       aq[:, c * 512: c * 512 + wdt],
                                       psF[sl][:, 0:wdt], AF.Exp,
                                       bias=smax[:, pos * 2 + 1: pos * 2 + 2],
                                       scale=SCALE,
                                       accum_out=rsp[:, c: c + 1]),
                                   [t_negmax] + (attnq_free[gp] if c == 0 else []),
                                   inc="SE")
                        psF_free[sl] = te
                    t_se = em.op("scalar", lambda e: e.drain(), [], inc="SE")
                    smax_free[pos] = [t_se]
                    idx = gp * 2 + pos
                    em.op("vector", lambda e, jc=jc, idx=idx: e.reduce_sum(
                        srec[:, idx * 2: idx * 2 + 1], rsp[:, 0:jc], axis=AX.X),
                        [t_se] + srec_free[idx], inc="VE")
                    em.op("vector", lambda e: e.drain(), [], inc="VE")
                    em.op("vector", lambda e, idx=idx: e.reciprocal(
                        srec[:, idx * 2 + 1: idx * 2 + 2],
                        srec[:, idx * 2: idx * 2 + 1]), [], inc="VE")
                    em.op("vector", lambda e: e.drain(), [], inc="VE")
                    em.op("vector", lambda e, idx=idx: e.tensor_copy(
                        rs_bf[:, idx: idx + 1], srec[:, idx * 2: idx * 2 + 1]), [], inc="VE")
                    t_rs = em.op("vector", lambda e: e.drain(), [], inc="VE")
                    tile_ctx[pos] = (t_se, t_rs)
                return {"g": g, "qev": qev, "tctx": tile_ctx, "aq": aq}

            def p2_stage45(ctx):
                nonlocal attnT_free, av_free, qgrp_free
                g = ctx["g"]
                gp = g % 2
                aq = ctx["aq"]
                tile_T_evs = {}
                rs_row_ev = {}
                for pos in range(2):
                    i = 2 * g + pos
                    idx = gp * 2 + pos
                    t_se, t_rs = ctx["tctx"][pos]
                    jobs = []
                    for jb in range(i + 1):
                        dst = attnT_g[:, jb * 256 + pos * 128: jb * 256 + pos * 128 + 128]
                        jobs.append((dst, aq[:, jb * 128:(jb + 1) * 128],
                                     [t_se] + (attnT_free if jb == 0 else [])))
                    jobs.append((rs_row[0:1, idx * 128:(idx + 1) * 128],
                                 rs_bf[:, idx: idx + 1], [t_rs]))
                    evs = transpose_batch(jobs)
                    attnT_free = []
                    rs_row_ev[pos] = evs[-1]
                    tile_T_evs[pos] = evs[:-1]
                attnq_free[gp] = tile_T_evs[0] + tile_T_evs[1]

                t_zero = em.op("gpsimd", lambda e, g=g: e.memset(
                    attnT_g[:, (2 * g + 1) * 256: (2 * g + 1) * 256 + 128], 0.0),
                    [ctx["qev"][0]], inc="GE")

                all_T = tile_T_evs[0] + tile_T_evs[1] + [t_zero]
                nj = 2 * g + 2
                av_ev = []
                av_slots = []
                for eb in range(4):
                    slot = psf_slot()
                    w = all_T + av_free + [v_ready[jb] for jb in range(nj)] + [psF_free[slot]]
                    av_free = []
                    for jb in range(nj):
                        em.op("tensor", lambda e, jb=jb, eb=eb, sl=slot, nj=nj: e.matmul(
                            psF[sl][:, 0:256],
                            v_tm[:, jb * E + eb * 128: jb * E + eb * 128 + 128],
                            attnT_g[:, jb * 256: jb * 256 + 256],
                            start=(jb == 0), stop=(jb == nj - 1)),
                            w if jb == 0 else [], inc=None)
                        w = []
                    av_slots.append(slot)
                t_av = pe_fence(psF[av_slots[-1]][:, 0:128])
                for eb, slot in enumerate(av_slots):
                    et = evac(av_fm[:, eb * 256:(eb + 1) * 256], psF[slot][:, 0:256], [t_av])
                    psF_free[slot] = et
                    av_ev.append(et)

                for pos in range(2):
                    ti = 2 * g + pos
                    p = pos
                    lxa = dma("sync", xa_stage[:, p * D:(p + 1) * D],
                              x_in[s, ti * 128:(ti + 1) * 128, :],
                              waits=xa_stage_free[p])
                    prj_toks = []
                    for oh in range(2):
                        slot = psf_slot()
                        w = av_ev + woutT_ready + [tok_ones, tok_brow, psF_free[slot]]
                        for eb in range(4):
                            em.op("tensor", lambda e, eb=eb, oh=oh, pos=pos, sl=slot: e.matmul(
                                psF[sl][:, 0:512],
                                av_fm[:, eb * 256 + pos * 128: eb * 256 + pos * 128 + 128],
                                woutT(eb, oh * 512, 512),
                                start=(eb == 0), stop=False),
                                w if eb == 0 else [])
                            w = []
                        idx = gp * 2 + pos
                        em.op("tensor", lambda e, oh=oh, sl=slot, idx=idx: e.matmul(
                            psF[sl][:, 0:512],
                            rs_row[0:1, idx * 128:(idx + 1) * 128],
                            b_row_bf[:, oh * 512: oh * 512 + 512],
                            start=False, stop=True), [rs_row_ev[pos]], inc=None)
                        prj_toks.append((slot, None))
                    t_prj = pe_fence(psF[prj_toks[-1][0]][:, 0:128])
                    prj_toks = [(sl, t_prj) for (sl, _unused) in prj_toks]
                    vts = []
                    idx = gp * 2 + pos
                    for oh, (slot, tokb) in enumerate(prj_toks):
                        vt = em.op("vector", lambda e, oh=oh, sl=slot, p=p, idx=idx: e.scalar_tensor_tensor(
                            xa_stage[:, p * D + oh * 512: p * D + oh * 512 + 512],
                            psF[sl][:, 0:512],
                            srec[:, idx * 2 + 1: idx * 2 + 2],
                            xa_stage[:, p * D + oh * 512: p * D + oh * 512 + 512],
                            ALU.mult, ALU.add),
                            [lxa, tokb], inc="VE")
                        psF_free[slot] = vt
                        vts.append(vt)
                    srec_free[idx] = [vts[-1]]
                    sxa = dma("gpsimd", scr_xa[ti], xa_stage[:, p * D:(p + 1) * D],
                              waits=vts, sem="ST")
                    xa_stage_free[p] = [sxa]
                    prj_done[ti] = sxa
                av_free = av_ev
                qgrp_free = ctx["qev"]

            prev_ctx = None
            for g in range(8):
                ctx = p2_stage12(g)
                if prev_ctx is not None:
                    p2_stage45(prev_ctx)
                prev_ctx = ctx
            p2_stage45(prev_ctx)

            last_store = None
            prev_th = []
            for hf in range(2):
                ha_ev = {}
                pair_i = [0]
                for hb in range(HB):
                    p = hb % 4
                    gug = prep_gu_state["stores"][-4:] if (s == 0 and hf == 0 and hb == 0) else []
                    lg = dma("sync", wg_buf[:, p * 1024:(p + 1) * 1024], scr_g[hb],
                             waits=wg_free[p] + gug)
                    lu = dma("sync", wu_buf[:, p * 1024:(p + 1) * 1024], scr_u[hb],
                             waits=wu_free[p] + gug)
                    pa = pair_i[0] % 3
                    pb = (pair_i[0] + 1) % 3
                    pair_i[0] += 2
                    wgw = [lg, psF_free[2 * pa], psF_free[2 * pa + 1]]
                    for dc in range(DC):
                        for th in range(2):
                            t0 = hf * 1024 + th * 512
                            em.op("tensor", lambda e, dc=dc, p=p, t0=t0, pa=pa, th=th: e.matmul(
                                psF[2 * pa + th][:, 0:512],
                                wg_buf[:, p * 1024 + dc * 128: p * 1024 + dc * 128 + 128],
                                xn_fm[:, dc * T + t0: dc * T + t0 + 512],
                                start=(dc == 0), stop=(dc == DC - 1)),
                                wgw, inc=None)
                            wgw = []
                    wuw = [lu, psF_free[2 * pb], psF_free[2 * pb + 1]]
                    for dc in range(DC):
                        for th in range(2):
                            t0 = hf * 1024 + th * 512
                            em.op("tensor", lambda e, dc=dc, p=p, t0=t0, pb=pb, th=th: e.matmul(
                                psF[2 * pb + th][:, 0:512],
                                wu_buf[:, p * 1024 + dc * 128: p * 1024 + dc * 128 + 128],
                                xn_fm[:, dc * T + t0: dc * T + t0 + 512],
                                start=(dc == 0), stop=(dc == DC - 1)),
                                wuw, inc=None)
                            wuw = []
                    tokf = pe_fence(psF[2 * pb + 1][:, 0:128])
                    t_sil = em.op("scalar", lambda e, pa=pa: e.activation(
                        sil_buf[:, 0:1024], psB3[pa][:, 0:1024],
                        AF.Silu, scale=DEQ), [tokf] + silup_free[0], inc="SE")
                    psF_free[2 * pa] = t_sil
                    psF_free[2 * pa + 1] = t_sil
                    t_h = em.op("vector", lambda e, hb=hb, pb=pb: e.scalar_tensor_tensor(
                        h_act_ap(hb, 0, 1024),
                        psB3[pb][:, 0:1024], DEQ,
                        sil_buf[:, 0:1024],
                        ALU.mult, ALU.mult),
                        [t_sil] + (arena_users if hb == 0 else []),
                        inc="VE")
                    psF_free[2 * pb] = t_h
                    psF_free[2 * pb + 1] = t_h
                    arena_users = []
                    silup_free[0] = [t_h]
                    ha_ev[(hb, 0)] = t_h
                    ha_ev[(hb, 1)] = t_h
                    wg_free[p] = [t_h]
                    wu_free[p] = [t_h]

                ha_all = [ha_ev[(hb, tc2)] for hb in range(HB) for tc2 in range(2)]
                for tbs in ([0, 1, 2], [3, 4, 5], [6, 7]):
                    mm_toks = {}
                    for hc in range(HB):
                        p = hc % 4
                        ld = dma("sync", wd_buf[:, p * 1024:(p + 1) * 1024], scr_d[hc],
                                 waits=wd_free[p])
                        chunk_last = None
                        for k, tb in enumerate(tbs):
                            for oh in range(2):
                                sl = k * 2 + oh
                                w = [ld] if (k == 0 and oh == 0) else []
                                if hc == 0:
                                    w = w + ha_all + [psF_free[sl]]
                                is_last_mm = (k == len(tbs) - 1 and oh == 1)
                                tokm = em.op("tensor",
                                             lambda e, hc=hc, tb=tb, oh=oh, hf=hf, p=p, sl=sl: e.matmul(
                                                 psF[sl][:, 0:512],
                                                 h_act_ap(hc, tb * 128, 128),
                                                 wd_buf[:, p * 1024 + oh * 512: p * 1024 + oh * 512 + 512],
                                                 start=(hc == 0), stop=(hc == HB - 1)),
                                             w, inc="PE" if (is_last_mm or hc == HB - 1) else None)
                                if hc == HB - 1:
                                    mm_toks[(tb, oh)] = tokm
                                if is_last_mm:
                                    chunk_last = tokm
                        wd_free[p] = [chunk_last]
                    t_dn = pe_fence(psF[2 * len(tbs) - 1][:, 0:128])
                    for kk in mm_toks:
                        mm_toks[kk] = t_dn
                    for k, tb in enumerate(tbs):
                        ti = hf * 8 + tb
                        p = ti % 2
                        lxa = dma("sync", xa_stage[:, p * D:(p + 1) * D], scr_xa[ti],
                                  waits=xa_stage_free[p] + [prj_done[ti]])
                        vts = []
                        for oh in range(2):
                            sl = k * 2 + oh
                            vt = em.op("vector", lambda e, sl=sl, oh=oh, p=p: e.scalar_tensor_tensor(
                                out_stage[:, p * D + oh * 512: p * D + oh * 512 + 512],
                                psF[sl][:, 0:512], DEQ,
                                xa_stage[:, p * D + oh * 512: p * D + oh * 512 + 512],
                                ALU.mult, ALU.add),
                                [lxa, mm_toks[(tb, oh)]] + out_stage_free[p], inc="VE")
                            out_stage_free[p] = []
                            psF_free[sl] = vt
                            vts.append(vt)
                        so = dma("gpsimd", out[s, ti * 128:(ti + 1) * 128, :],
                                 out_stage[:, p * D:(p + 1) * D], waits=vts, sem="ST")
                        out_stage_free[p] = [so]
                        xa_stage_free[p] = [so]
                        last_store = so
            arena_users = [last_store]

        for i in range(N_LANES["ST"]):
            tot = em.n(f"ST{i}")
            em.op("gpsimd", lambda e, i=i, tot=tot: e.wait_ge(sems[f"ST{i}"], tot))

        with nc.Block() as block:
            def go_factory(eng_name):
                is_pe = eng_name == "tensor"
                defer_drain = eng_name in ("vector", "scalar", "gpsimd")
                def go(eng):
                    for waits, fn, tok, kind in em.ops[eng_name]:
                        for sname, val in waits:
                            eng.wait_ge(sems[sname], val)
                        inst = fn(eng)
                        if tok is not None:
                            inst.then_inc(sems[tok[0]], tok[1])
                return go
            block.sync(go_factory("sync"))
            block.scalar(go_factory("scalar"))
            block.vector(go_factory("vector"))
            block.tensor(go_factory("tensor"))
            block.gpsimd(go_factory("gpsimd"))
    return nc


_NC_CACHE = None


def kernel(L0_attn_x, L0_attn_w_qkv, L0_attn_w_out, L0_attn_b_out,
           L0_ffn_w_gate, L0_ffn_w_up, L0_ffn_w_down):
    global _NC_CACHE
    x = np.ascontiguousarray(L0_attn_x, dtype=np.float32)
    wqkv = np.ascontiguousarray(L0_attn_w_qkv, dtype=np.float32)
    wout = np.ascontiguousarray(L0_attn_w_out, dtype=np.float32)
    bout = np.ascontiguousarray(L0_attn_b_out, dtype=np.float32).reshape(1, D)
    wg = np.ascontiguousarray(L0_ffn_w_gate, dtype=np.int32)
    wu = np.ascontiguousarray(L0_ffn_w_up, dtype=np.int32)
    wd = np.ascontiguousarray(L0_ffn_w_down, dtype=np.int32)

    if _NC_CACHE is None:
        _NC_CACHE = build_nc()
    nc = _NC_CACHE
    in_maps = []
    for c in range(8):
        in_maps.append({
            "x_in": np.ascontiguousarray(x[4 * c: 4 * (c + 1)]),
            "w_qkv": wqkv, "w_out": wout, "b_out": bout,
            "w_gate": wg, "w_up": wu, "w_down": wd,
        })
    res = run_bass_kernel_spmd(nc, in_maps, core_ids=list(range(8)))
    return np.concatenate([r["out"] for r in res.results], axis=0)
